# version 01
# baseline (speedup 1.0000x reference)
import numpy as np
import ml_dtypes
from contextlib import ExitStack

import concourse.bass as bass
import concourse.mybir as mybir
import bass_rust
from concourse.bass_utils import run_bass_kernel_spmd

F32 = mybir.dt.float32
BF16 = mybir.dt.bfloat16
AF = mybir.ActivationFunctionType
ALU = mybir.AluOpType
AX = mybir.AxisListType

RD, ND, VD = 64, 64, 128

CFG = dict(D=2048, DFF=8192, S=2048, QL=1536, CKV=512, H=16, TSPLIT=1280, GN=512)

ENGS = ("te", "dve", "act", "pool", "sync")
NSLOT = 4
QS = ("sQs", "sQa", "sQp")
SEMS = ("sT", "sV", "sA") + tuple(f"{q}{i}" for q in QS for i in range(NSLOT))
Q_ENG = {"sQs": "sync", "sQa": "act", "sQp": "pool"}


PROD = {"sT": "te", "sV": "dve", "sA": "act"}


class Rec:

    def __init__(self, sem):
        self.sem = sem
        self.q = {e: [] for e in ENGS}
        self.c = {s: 0 for s in SEMS}
        self.last_mark = {}
        self.trace = []

    def mark(self, s):
        v = self.c[s]
        if v > 0 and self.last_mark.get(s) != v:
            self.q[PROD[s]].append(lambda E, s=s, v=v: E.wait_ge(self.sem[s], v))
            self.last_mark[s] = v
        return v

    def wait(self, eng, s, v):
        if v > 0:
            if s in PROD and eng != PROD[s] and v == self.c[s]:
                self.mark(s)
            self.trace.append(("wait", eng, s, v))
            self.q[eng].append(lambda E, s=s, v=v: E.wait_ge(self.sem[s], v))

    def wq(self, eng, qs, slot, v):
        self.wait(eng, f"{qs}{slot}", v)

    def qc(self, qs, slot):
        return self.c[f"{qs}{slot}"]

    def op(self, eng, s, fn, k=1):
        def g(E, fn=fn, s=s, k=k):
            fn(E).then_inc(self.sem[s], k)
        self.q[eng].append(g)
        self.c[s] += k
        self.trace.append(("op", eng, s, self.c[s], getattr(fn, "tag", None)))
        return self.c[s]

    def dma(self, qs, slot, out_ap, in_ap, wait_first=True):
        s = f"{qs}{slot}"
        eng = Q_ENG[qs]
        if wait_first and self.c[s] > 0:
            self.wait(eng, s, self.c[s])
        def g(E, o=out_ap, i=in_ap, s=s):
            E.dma_start(out=o, in_=i).then_inc(self.sem[s], 16)
        self.q[eng].append(g)
        self.c[s] += 16
        return self.c[s]

    def dma_f(self, qs, slot, fn, wait_first=True):
        s = f"{qs}{slot}"
        eng = Q_ENG[qs]
        if wait_first and self.c[s] > 0:
            self.wait(eng, s, self.c[s])
        def g(E, fn=fn, s=s):
            fn(E).then_inc(self.sem[s], 16)
        self.q[eng].append(g)
        self.c[s] += 16
        return self.c[s]

    def snap(self):
        return dict(self.c)

    def barrier(self):
        for eng, s in (("te", "sT"), ("dve", "sV"), ("act", "sA")):
            self.op(eng, s, lambda E: E.nop())
        snap = self.snap()
        for eng in ENGS:
            for s, v in snap.items():
                self.wait(eng, s, v)


def ceil_div(a, b):
    return (a + b - 1) // b


def _role_params(cfg, role):
    S, TSPLIT = cfg["S"], cfg["TSPLIT"]
    own0 = 0 if role == 0 else TSPLIT
    own_n = TSPLIT if role == 0 else S - TSPLIT
    return own0, own_n, own0 + own_n


def build_nc(cfg):
    D, DFF, S = cfg["D"], cfg["DFF"], cfg["S"]
    QL, CKV, H = cfg["QL"], cfg["CKV"], cfg["H"]
    TSPLIT, GN = cfg["TSPLIT"], cfg["GN"]
    KD, FD, QLD, CKVD = D // 128, DFF // 128, QL // 128, CKV // 128
    NT = S // 128
    HC = H * VD
    HCD = HC // 128
    sm_scale = float(ND + RD) ** -0.5

    nc = bass.Bass("TRN2", target_bir_lowering=False)

    x_in = nc.dram_tensor("x_in", [S, D], F32, kind="ExternalInput").ap()
    wfcT = nc.dram_tensor("wfcT", [KD, FD // 4, 128, 512], F32, kind="ExternalInput").ap()
    wprT = nc.dram_tensor("wprT", [FD, KD // 4, 128, 512], F32, kind="ExternalInput").ap()
    wqa = nc.dram_tensor("wqa", [KD, QLD, 128, 128], F32, kind="ExternalInput").ap()
    wqb = nc.dram_tensor("wqb", [QLD, H, 128, 128], F32, kind="ExternalInput").ap()
    wkva = nc.dram_tensor("wkva", [KD, CKVD + 1, 128, 128], F32, kind="ExternalInput").ap()
    wkvk = nc.dram_tensor("wkvk", [CKVD, H, 128, ND], F32, kind="ExternalInput").ap()
    wkvv = nc.dram_tensor("wkvv", [CKVD, H // 4, 128, 4 * VD], F32, kind="ExternalInput").ap()
    wo_in = nc.dram_tensor("wo_in", [HCD, KD // 4, 128, 512], F32, kind="ExternalInput").ap()
    ident_d = nc.dram_tensor("ident_c", [128, 128], BF16, kind="ExternalInput").ap()
    mask_d = nc.dram_tensor("mask_c", [128, 128], F32, kind="ExternalInput").ap()
    ropeC_d = nc.dram_tensor("ropeC_c", [RD, S], BF16, kind="ExternalInput").ap()
    ropeS_d = nc.dram_tensor("ropeS_c", [RD, S], BF16, kind="ExternalInput").ap()

    out_d = nc.dram_tensor("out", [TSPLIT, D], F32, kind="ExternalOutput").ap()
    resid_d = nc.dram_tensor("resid_sc", [S, D], F32, kind="Internal").ap()
    ternfc_d = nc.dram_tensor("ternfc", [KD, FD // 4, 128, 512], BF16, kind="Internal").ap()
    ternpr_d = nc.dram_tensor("ternpr", [FD, KD // 4, 128, 512], BF16, kind="Internal").ap()
    v_sc = nc.dram_tensor("v_sc", [NT, H, 128, VD], BF16, kind="Internal").ap()
    attn_sc = nc.dram_tensor("attn_sc", [NT, H, 128, VD], BF16, kind="Internal").ap()
    wo_bf = nc.dram_tensor("wo_bf", [HCD, KD // 4, 128, 512], BF16, kind="Internal").ap()
    gsc = nc.dram_tensor("gamma_sc", [128, 4], F32, kind="Internal").ap()

    TOP = ExitStack()
    with TOP:
        sem = {s: TOP.enter_context(nc.semaphore(s)) for s in SEMS}
        sb = lambda n, shp, dt: TOP.enter_context(nc.sbuf_tensor(n, shp, dt))

        ident = sb("identS", [128, 128], BF16)
        maskS = sb("maskS", [128, 128], F32)
        stat = sb("stat", [128, 128], F32)
        scol = sb("scol", [128, 32], F32)
        gcol = sb("gcol", [128, 16], F32)
        gacc = sb("gacc", [128, 224], F32)
        wring = [sb(f"wring{i}", [128, 512], BF16) for i in range(16)]
        wf32 = [sb(f"wf32{i}", [128, 1024], F32) for i in range(2)]
        p1buf = sb("p1buf", [128, 512], BF16)
        rtile = [sb(f"rtile{i}", [128, 512], F32) for i in range(4)]
        xt = [sb(f"xt{i}", [128, D], F32) for i in range(2)]
        xnb = sb("xnb", [128, D], BF16)
        xhat = sb("xhat", [128, D], BF16)

        ps = [TOP.enter_context(nc.psum_tensor(f"pb{i}", [128, 512], F32)) for i in range(4)]
        pmisc = TOP.enter_context(nc.psum_tensor("pmisc", [128, 512], F32))
        psbf = TOP.enter_context(nc.psum_tensor("psbf", [128, 1024], BF16))
        pav = TOP.enter_context(nc.psum_tensor("pav", [128, 512], F32))

        recs = [Rec(sem), Rec(sem)]
        roles = [(_role_params(cfg, r)) for r in (0, 1)]

        def transpose_quads(R, jobs, src_fn, dst_fn, pslot_base=0):
            nq = jobs // 4
            assert jobs % 4 == 0
            ev = {}
            for q in range(nq):
                if q >= 1:
                    e, cnt = ev[q - 1]
                    R.wait("te", "sA" if e == "act" else "sV", cnt)
                for j4 in range(4):
                    j = 4 * q + j4
                    R.op("te", "sT", lambda E, j=j, j4=j4: E.transpose(
                        out=psbf[:, pslot_base + 128 * j4:pslot_base + 128 * (j4 + 1)],
                        in_=src_fn(j), identity=ident[:]))
                eng = "act" if q % 2 == 0 else "dve"
                R.wait(eng, "sT", R.c["sT"])
                if eng == "act":
                    cnt = R.op("act", "sA", lambda E, q=q: E.activation(
                        out=dst_fn(q), in_=psbf[:, pslot_base:pslot_base + 512],
                        func=AF.Copy))
                else:
                    cnt = R.op("dve", "sV", lambda E, q=q: E.tensor_copy(
                        out=dst_fn(q), in_=psbf[:, pslot_base:pslot_base + 512]))
                ev[q] = (eng, cnt)
            e, cnt = ev[nq - 1]
            R.wait("te", "sA" if e == "act" else "sV", cnt)
            return R.snap()

        def ln_stats(R, src_t, width, eps_col, fuse_double):
            nblk = ceil_div(width, 512)
            for b in range(nblk):
                R.op("dve", "sV", lambda E, b=b: E.bn_stats(
                    out=stat[:, 6 * b:6 * b + 6], in_=src_t[:, 512 * b:512 * (b + 1)]))
            R.wait("dve", "sV", R.c["sV"])
            R.op("dve", "sV", lambda E: E.bn_aggr(out=scol[:, 0:2], in_=stat[:, 0:6 * nblk]))
            R.wait("act", "sV", R.c["sV"])
            R.op("act", "sA", lambda E: E.activation(
                out=scol[:, 2:3], in_=scol[:, 1:2], func=AF.Sqrt,
                bias=gcol[:, eps_col:eps_col + 1], scale=1.0))
            R.wait("dve", "sA", R.c["sA"])
            R.op("dve", "sV", lambda E: E.reciprocal(
                out=scol[:, 2:3], in_=scol[:, 2:3]))
            R.wait("dve", "sV", R.c["sV"])
            if fuse_double:
                R.op("dve", "sV", lambda E: E.tensor_tensor(
                    out=scol[:, 3:4], in0=scol[:, 1:2], in1=scol[:, 2:3], op=ALU.mult))
                R.wait("dve", "sV", R.c["sV"])
                R.op("dve", "sV", lambda E: E.tensor_tensor(
                    out=scol[:, 3:4], in0=scol[:, 3:4], in1=scol[:, 2:3], op=ALU.mult))
                R.wait("act", "sV", R.c["sV"])
                R.op("act", "sA", lambda E: E.activation(
                    out=scol[:, 4:5], in_=scol[:, 3:4], func=AF.Sqrt,
                    bias=gcol[:, 15:16], scale=1.0))
                R.wait("dve", "sA", R.c["sA"])
                R.op("dve", "sV", lambda E: E.reciprocal(
                    out=scol[:, 4:5], in_=scol[:, 4:5]))
                R.wait("dve", "sV", R.c["sV"])
                R.op("dve", "sV", lambda E: E.tensor_tensor(
                    out=scol[:, 5:6], in0=scol[:, 2:3], in1=scol[:, 4:5], op=ALU.mult))
            R.op("dve", "sV", lambda E: E.tensor_scalar(
                out=scol[:, 6:7], in0=scol[:, 0:1], scalar1=-1.0, scalar2=None, op0=ALU.mult))
            R.wait("dve", "sV", R.c["sV"])

        def absclip(R, in_ap, out_ap, rscale_col, negm_col):
            R.op("dve", "sV", lambda E: E.tensor_scalar(
                out=out_ap, in0=in_ap, scalar1=scol[:, negm_col:negm_col + 1],
                scalar2=scol[:, rscale_col:rscale_col + 1], op0=ALU.add, op1=ALU.mult))
            R.wait("dve", "sV", R.c["sV"])
            R.op("dve", "sV", lambda E: E.reduce_max(
                out=scol[:, 7:8], in_=out_ap, axis=AX.X, apply_absolute_value=True))
            R.wait("dve", "sV", R.c["sV"])
            R.op("dve", "sV", lambda E: E.tensor_tensor(
                out=scol[:, 8:9], in0=scol[:, 7:8], in1=gcol[:, 0:1], op=ALU.mult))
            R.wait("dve", "sV", R.c["sV"])
            R.op("dve", "sV", lambda E: E.tensor_scalar(
                out=scol[:, 9:10], in0=scol[:, 8:9], scalar1=-1.0, scalar2=None, op0=ALU.mult))
            R.wait("dve", "sV", R.c["sV"])
            R.op("dve", "sV", lambda E: E.tensor_scalar(
                out=out_ap, in0=out_ap, scalar1=scol[:, 8:9], scalar2=scol[:, 9:10],
                op0=ALU.min, op1=ALU.max))
            R.wait("dve", "sV", R.c["sV"])

        A1 = ExitStack()
        xhatT = A1.enter_context(nc.sbuf_tensor("xhatT", [128, KD * GN], BF16))
        hbuf = [A1.enter_context(nc.sbuf_tensor(f"h{i}", [128, DFF], BF16)) for i in range(4)]
        hhatT = A1.enter_context(nc.sbuf_tensor("hhatT", [128, FD * GN], BF16))

        def emit_p0(R):
            R.dma("sQs", 0, ident[:], ident_d[:])
            R.dma("sQs", 1, maskS[:], mask_d[:])
            R.op("dve", "sV", lambda E: E.memset(gcol[:, 0:1], -1e-6))
            R.wait("dve", "sV", R.c["sV"])
            R.op("dve", "sV", lambda E: E.tensor_scalar(
                out=gcol[:, 0:1], in0=gcol[:, 0:1], scalar1=128.0, scalar2=1.0 / 128.0,
                op0=ALU.add, op1=ALU.mult))
            R.op("dve", "sV", lambda E: E.memset(gcol[:, 14:15], 1e-5))
            R.op("dve", "sV", lambda E: E.memset(gcol[:, 15:16], 1e-6))
            R.wait("act", "sV", R.c["sV"])
            sfc = D // 512
            nfc_bl = (KD * FD // 4) // sfc
            npr_bl = (FD * KD // 4) // 2
            wfc_fl = wfcT.rearrange("a b p f -> (a b) p f")
            wpr_fl = wprT.rearrange("a b p f -> (a b) p f")
            adone = {}
            for i in range(nfc_bl):
                b = xt[i % 2]
                if i >= 2:
                    R.wait("sync", "sA", adone[i - 2])
                R.dma("sQs", i % 2, b[:],
                      wfc_fl[sfc * i:sfc * i + sfc].rearrange("a p f -> p a f"))
                R.wq("act", "sQs", i % 2, R.qc("sQs", i % 2))
                adone[i] = R.op("act", "sA", lambda E, b=b, i=i: E.activation(
                    out=b[:], in_=b[:], func=AF.Abs, accum_out=gacc[:, i:i + 1]))
                R.mark("sA")
            vdone = {}
            for i in range(npr_bl):
                b = wf32[i % 2]
                if i >= 2:
                    R.wait("act", "sV", vdone[i - 2])
                R.dma("sQa", i % 2, b[:], wpr_fl[2 * i:2 * i + 2].rearrange("a p f -> p a f"))
                R.wq("dve", "sQa", i % 2, R.qc("sQa", i % 2))
                vdone[i] = R.op("dve", "sV", lambda E, b=b, i=i: E.reduce_sum(
                    out=gacc[:, 64 + i:64 + i + 1], in_=b[:], axis=AX.X,
                    apply_absolute_value=True))
                R.mark("sV")
            R.wait("dve", "sA", R.c["sA"])
            R.wait("dve", "sV", R.c["sV"])
            R.op("dve", "sV", lambda E: E.reduce_sum(
                out=gcol[:, 1:2], in_=gacc[:, 0:nfc_bl], axis=AX.X))
            R.op("dve", "sV", lambda E: E.reduce_sum(
                out=gcol[:, 2:3], in_=gacc[:, 64:64 + npr_bl], axis=AX.X))
            R.wait("sync", "sV", R.c["sV"])
            R.dma("sQs", 3, gsc[:, 0:2], gcol[:, 1:3])
            def _bc_dma(E, o, i):
                with nc.allow_non_contiguous_dma(reason="one-time gamma reduce"):
                    return E.dma_start(out=o, in_=i)
            R.dma_f("sQs", 3, lambda E: _bc_dma(
                E, stat[0:2, 0:128], gsc[:, 0:2].rearrange("p c -> c p")))
            R.wq("dve", "sQs", 3, R.qc("sQs", 3))
            R.op("dve", "sV", lambda E: E.reduce_sum(
                out=stat[0:2, 127:128], in_=stat[0:2, 0:128], axis=AX.X))
            R.wait("sync", "sV", R.c["sV"])
            R.dma_f("sQs", 3, lambda E: _bc_dma(
                E, gsc[0:1, 2:4], stat[0:2, 127:128]))
            R.dma_f("sQs", 3, lambda E: _bc_dma(
                E, gcol[:, 4:5], gsc[0:1, 2:3].partition_broadcast(128)))
            R.dma_f("sQs", 3, lambda E: _bc_dma(
                E, gcol[:, 5:6], gsc[0:1, 3:4].partition_broadcast(128)))
            R.wq("dve", "sQs", 3, R.qc("sQs", 3))
            nel = float(KD * FD) * 128.0 * 128.0
            R.op("dve", "sV", lambda E: E.tensor_scalar(
                out=gcol[:, 6:7], in0=gcol[:, 4:5], scalar1=1.0 / nel, scalar2=None,
                op0=ALU.mult))
            R.op("dve", "sV", lambda E: E.tensor_scalar(
                out=gcol[:, 7:8], in0=gcol[:, 5:6], scalar1=1.0 / nel, scalar2=None,
                op0=ALU.mult))
            R.wait("dve", "sV", R.c["sV"])
            R.op("dve", "sV", lambda E: E.tensor_scalar(
                out=gcol[:, 8:9], in0=gcol[:, 6:7], scalar1=1e-6, scalar2=0.5,
                op0=ALU.add, op1=ALU.mult))
            R.op("dve", "sV", lambda E: E.tensor_scalar(
                out=gcol[:, 9:10], in0=gcol[:, 7:8], scalar1=1e-6, scalar2=0.5,
                op0=ALU.add, op1=ALU.mult))
            R.wait("dve", "sV", R.c["sV"])
            R.op("dve", "sV", lambda E: E.tensor_scalar(
                out=gcol[:, 10:11], in0=gcol[:, 8:9], scalar1=-1.0, scalar2=None, op0=ALU.mult))
            R.op("dve", "sV", lambda E: E.tensor_scalar(
                out=gcol[:, 11:12], in0=gcol[:, 9:10], scalar1=-1.0, scalar2=None, op0=ALU.mult))
            R.op("dve", "sV", lambda E: E.tensor_scalar(
                out=gcol[:, 12:13], in0=gcol[:, 6:7], scalar1=-1.0, scalar2=None, op0=ALU.mult))
            R.op("dve", "sV", lambda E: E.tensor_scalar(
                out=gcol[:, 13:14], in0=gcol[:, 7:8], scalar1=-0.5, scalar2=None, op0=ALU.mult))
            R.barrier()

        def emit_ffn_group(R, g0, gn, make_tern, src_ap, dst_mode):
            gt = gn // 128
            xhatT_v = xhatT.rearrange("p (k g) -> p k g", g=GN)
            hhatT_v = hhatT.rearrange("p (k g) -> p k g", g=GN)
            for t in range(gt):
                xb = xt[t % 2]
                if t >= 2:
                    R.wait("act", "sT", R.c["sT"])
                    R.wait("act", "sV", R.c["sV"])
                R.dma("sQa", t % 2, xb[:], src_ap[g0 + 128 * t:g0 + 128 * (t + 1), :])
                R.wq("dve", "sQa", t % 2, R.qc("sQa", t % 2))
                ln_stats(R, xb[:], D, 14, True)
                absclip(R, xb[:], xhat[:], 5, 6)
                R.wait("te", "sV", R.c["sV"])
                transpose_quads(
                    R, KD,
                    lambda j: xhat[:, 128 * j:128 * (j + 1)],
                    lambda q, t=t: xhatT_v[:, 4 * q:4 * q + 4, 128 * t:128 * (t + 1)])
            wuse = {}
            R.wait("sync", "sT", R.mark("sT"))
            for _sl in range(4):
                R.wq("sync", "sQp", _sl, R.qc("sQp", _sl))
            R.wait("sync", "sV", R.mark("sV"))
            cb_mark = {}
            for cb in range(FD // 4):
                for k in range(KD):
                    j = cb * KD + k
                    wb = wring[j % 16]
                    if make_tern:
                        fb = wf32[j % 2]
                        if j >= 2:
                            R.wait("sync", "sV", wuse.get(("t", j - 2), 0))
                        R.dma("sQs", j % 2, fb[:, 0:512], wfcT[k, cb])
                        R.wq("dve", "sQs", j % 2, R.qc("sQs", j % 2))
                        if j >= 16:
                            R.wait("dve", "sT", R.last_mark.get("sT", 0))
                            _q, _sl, _v = wuse[("s", j - 16)]
                            R.wq("dve", _q, _sl, _v)
                        R.wait("dve", "sV", R.c["sV"])
                        R.op("dve", "sV", lambda E, fb=fb: E.tensor_scalar(
                            out=p1buf[:, 0:512], in0=fb[:, 0:512], scalar1=gcol[:, 8:9],
                            scalar2=None, op0=ALU.is_gt))
                        R.wait("dve", "sV", R.c["sV"])
                        wuse[("t", j)] = R.op("dve", "sV",
                            lambda E, fb=fb, wb=wb: E.scalar_tensor_tensor(
                                out=wb[:, 0:512], in0=fb[:, 0:512], scalar=gcol[:, 10:11],
                                in1=p1buf[:, 0:512], op0=ALU.is_lt, op1=ALU.subtract))
                        R.mark("sV")
                        R.wait("pool", "sV", R.c["sV"])
                        wuse[("s", j)] = ("sQp", j % 4, R.dma("sQp", j % 4, ternfc_d[k, cb], wb[:, 0:512]))
                        R.wait("te", "sV", R.c["sV"])
                    else:
                        if j >= 16:
                            R.wait("sync", "sT", R.last_mark.get("sT", 0))
                        R.dma("sQs", j % 4, wb[:, 0:512], ternfc_d[k, cb])
                        R.wq("te", "sQs", j % 4, R.qc("sQs", j % 4))
                    for t in range(gt):
                        R.op("te", "sT", lambda E, wb=wb, t=t, k=k: E.matmul(
                            out=ps[t][:], lhsT=xhatT_v[:, k, 128 * t:128 * (t + 1)],
                            rhs=wb[:, 0:512], start=(k == 0), stop=(k == KD - 1)))
                    if k == KD - 1:
                        R.mark("sT")
                cb_mark[cb] = R.mark("sT")
                R.wait("act", "sT", R.c["sT"])
                for t in range(gt):
                    R.op("act", "sA", lambda E, t=t, cb=cb: E.activation(
                        out=hbuf[t][:, 512 * cb:512 * (cb + 1)], in_=ps[t][:],
                        func=AF.Gelu, scale=gcol[:, 12:13]))
                R.wait("te", "sA", R.c["sA"])
            R.wait("dve", "sA", R.c["sA"])
            for t in range(gt):
                hb = hbuf[t]
                ln_stats(R, hb[:], DFF, 15, False)
                absclip(R, hb[:], hb[:], 2, 6)
                R.wait("te", "sV", R.c["sV"])
                transpose_quads(
                    R, FD,
                    lambda j, hb=hb: hb[:, 128 * j:128 * (j + 1)],
                    lambda q, t=t: hhatT_v[:, 4 * q:4 * q + 4, 128 * t:128 * (t + 1)],
                    pslot_base=512)
            R.wait("sync", "sT", R.mark("sT"))
            for _sl in range(4):
                R.wq("sync", "sQp", _sl, R.qc("sQp", _sl))
            R.wait("sync", "sV", R.mark("sV"))
            if make_tern:
                for _sl in range(4):
                    R.wq("dve", "sQp", _sl, R.qc("sQp", _sl))
            ob_mark = {}
            for ob in range(D // 512):
                R.wait("act", "sV", R.c["sV"])
                if dst_mode[0] == "final":
                    srcr = resid_d
                    r_src0 = g0
                else:
                    srcr = src_ap
                    r_src0 = g0
                c_r = []
                for t in range(gt):
                    R.dma("sQa", t, rtile[t][:],
                          srcr[r_src0 + 128 * t:r_src0 + 128 * (t + 1), 512 * ob:512 * (ob + 1)])
                    c_r.append(R.qc("sQa", t))
                for k2 in range(FD):
                    j = ob * FD + k2
                    wb = wring[j % 16]
                    if make_tern:
                        fb = wf32[j % 2]
                        if j >= 2:
                            R.wait("sync", "sV", wuse.get(("pt", j - 2), 0))
                        R.dma("sQs", j % 2, fb[:, 0:512], wprT[k2, ob])
                        R.wq("dve", "sQs", j % 2, R.qc("sQs", j % 2))
                        if j >= 16:
                            R.wait("dve", "sT", R.last_mark.get("sT", 0))
                            _q, _sl, _v = wuse[("ps", j - 16)]
                            R.wq("dve", _q, _sl, _v)
                        R.wait("dve", "sV", R.c["sV"])
                        R.op("dve", "sV", lambda E, fb=fb: E.tensor_scalar(
                            out=p1buf[:, 0:512], in0=fb[:, 0:512], scalar1=gcol[:, 9:10],
                            scalar2=None, op0=ALU.is_gt))
                        R.wait("dve", "sV", R.c["sV"])
                        wuse[("pt", j)] = R.op("dve", "sV",
                            lambda E, fb=fb, wb=wb: E.scalar_tensor_tensor(
                                out=wb[:, 0:512], in0=fb[:, 0:512], scalar=gcol[:, 11:12],
                                in1=p1buf[:, 0:512], op0=ALU.is_lt, op1=ALU.subtract))
                        R.mark("sV")
                        R.wait("pool", "sV", R.c["sV"])
                        wuse[("ps", j)] = ("sQp", j % 4, R.dma("sQp", j % 4, ternpr_d[k2, ob], wb[:, 0:512]))
                        R.wait("te", "sV", R.c["sV"])
                    else:
                        if j >= 16:
                            R.wait("sync", "sT", R.last_mark.get("sT", 0))
                        R.dma("sQs", j % 4, wb[:, 0:512], ternpr_d[k2, ob])
                        R.wq("te", "sQs", j % 4, R.qc("sQs", j % 4))
                    for t in range(gt):
                        R.op("te", "sT", lambda E, wb=wb, t=t, k2=k2: E.matmul(
                            out=ps[t][:], lhsT=hhatT_v[:, k2, 128 * t:128 * (t + 1)],
                            rhs=wb[:, 0:512], start=(k2 == 0), stop=(k2 == FD - 1)))
                    if k2 % 4 == 3:
                        R.mark("sT")
                ob_mark[ob] = R.mark("sT")
                R.wait("dve", "sT", R.c["sT"])
                for t in range(gt):
                    R.wq("dve", "sQa", t, c_r[t])
                for t in range(gt):
                    R.op("dve", "sV", lambda E, t=t: E.scalar_tensor_tensor(
                        out=rtile[t][:], in0=ps[t][:], scalar=gcol[:, 13:14], in1=rtile[t][:],
                        op0=ALU.mult, op1=ALU.add))
                R.wait("act", "sV", R.c["sV"])
                for t in range(gt):
                    if dst_mode[0] == "resid":
                        R.dma("sQa", t, resid_d[g0 + 128 * t:g0 + 128 * (t + 1),
                                                512 * ob:512 * (ob + 1)], rtile[t][:],
                              wait_first=False)
                    else:
                        r0 = dst_mode[1]
                        R.dma("sQa", t, out_d[g0 - r0 + 128 * t:g0 - r0 + 128 * (t + 1),
                                              512 * ob:512 * (ob + 1)], rtile[t][:],
                              wait_first=False)

        def groups_of(t0, n):
            out = []
            a = t0
            while a < t0 + n:
                g = min(GN, t0 + n - a)
                out.append((a, g))
                a += g
            return out

        dbg = cfg.get("dbg")
        for role in (0, 1):
            R = recs[role]
            own0, own_n, keys_n = roles[role]
            emit_p0(R)
            for gi, (g0, gnn) in enumerate(groups_of(0, keys_n)):
                emit_ffn_group(R, g0, gnn, gi == 0, x_in, ("resid",))
                if gi == 0:
                    R.barrier()
            R.barrier()
        A1.close()

        def dump_resid(R, row0):
            for tdmp in range(TSPLIT // 128):
                R.dma("sQa", tdmp % 4, out_d[128 * tdmp:128 * (tdmp + 1), :],
                      resid_d[row0 + 128 * tdmp:row0 + 128 * (tdmp + 1), :])
            R.barrier()

        if dbg in ("p1", "p1r"):
            for role in (0, 1):
                R = recs[role]
                own0, own_n, keys_n = roles[role]
                if dbg == "p1":
                    dump_resid(R, own0 if role == 0 else S - TSPLIT)
                else:
                    dump_resid(R, 0)
            finish(nc, recs)
            return nc

        A2a = ExitStack()
        ropeC = A2a.enter_context(nc.sbuf_tensor("ropeCS", [RD, S], BF16))
        ropeS = A2a.enter_context(nc.sbuf_tensor("ropeSS", [RD, S], BF16))
        qlatT = A2a.enter_context(nc.sbuf_tensor("qlatT", [128, QLD * TSPLIT], BF16))
        ckvT = A2a.enter_context(nc.sbuf_tensor("ckvT", [128, CKVD * S], BF16))
        kpeRT = A2a.enter_context(nc.sbuf_tensor("kpeRT", [RD, S], BF16))
        A2b = ExitStack()
        xnT = A2b.enter_context(nc.sbuf_tensor("xnT", [128, KD * S], BF16))
        wcol = A2b.enter_context(nc.sbuf_tensor("wcol", [128, 16 * 128], BF16))
        kpeT = A2b.enter_context(nc.sbuf_tensor("kpeT", [RD, S], F32))
        kpesw = A2b.enter_context(nc.sbuf_tensor("kpesw", [RD, S], F32))

        for role in (0, 1):
            R = recs[role]
            own0, own_n, keys_n = roles[role]
            nt_keys = keys_n // 128
            xnT_v = xnT.rearrange("p (k s) -> p k s", s=S)
            ckvT_v = ckvT.rearrange("p (k s) -> p k s", s=S)
            qlatT_v = qlatT.rearrange("p (k s) -> p k s", s=TSPLIT)

            R.dma("sQs", 0, ropeC[:], ropeC_d[:])
            c_ropeC = R.qc("sQs", 0)
            R.dma("sQs", 1, ropeS[:], ropeS_d[:])
            c_ropeS = R.qc("sQs", 1)

            for tt in range(nt_keys):
                xb = xt[tt % 2]
                if tt >= 2:
                    R.wait("act", "sT", R.c["sT"])
                    R.wait("act", "sV", R.c["sV"])
                R.dma("sQa", tt % 2, xb[:], resid_d[128 * tt:128 * (tt + 1), :])
                R.wq("dve", "sQa", tt % 2, R.qc("sQa", tt % 2))
                ln_stats(R, xb[:], D, 14, False)
                R.op("dve", "sV", lambda E, xb=xb: E.tensor_scalar(
                    out=xhat[:], in0=xb[:], scalar1=scol[:, 6:7], scalar2=scol[:, 2:3],
                    op0=ALU.add, op1=ALU.mult))
                R.wait("te", "sV", R.c["sV"])
                transpose_quads(
                    R, KD,
                    lambda j: xhat[:, 128 * j:128 * (j + 1)],
                    lambda q, tt=tt: xnT_v[:, 4 * q:4 * q + 4, 128 * tt:128 * (tt + 1)])

            def proj_cols(wsrc, mc, n_k, rhs_view, rhs_n, evac):
                R.wait("dve", "sT", R.c["sT"])
                for k in range(n_k):
                    fb = rtile[k % 2]
                    R.wait("act", "sV", R.c["sV"])
                    R.dma("sQa", k % 2, fb[:, 0:128], wsrc[k, mc])
                    R.wq("dve", "sQa", k % 2, R.qc("sQa", k % 2))
                    R.op("dve", "sV", lambda E, fb=fb, k=k: E.tensor_copy(
                        out=wcol[:, 128 * k:128 * (k + 1)], in_=fb[:, 0:128]))
                R.wait("te", "sV", R.c["sV"])
                nblk = ceil_div(rhs_n, 512)
                for blk in range(nblk):
                    b0 = 512 * blk
                    bn = min(512, rhs_n - b0)
                    for k in range(n_k):
                        R.op("te", "sT", lambda E, k=k, b0=b0, bn=bn: E.matmul(
                            out=pmisc[:, 0:bn], lhsT=wcol[:, 128 * k:128 * (k + 1)],
                            rhs=rhs_view(k, b0, bn), start=(k == 0), stop=(k == n_k - 1)))
                    evac(blk, b0, bn)

            for mc in range(CKVD + 1):
                def evac_ckv(blk, b0, bn, mc=mc):
                    R.wait("dve", "sT", R.c["sT"])
                    if mc < CKVD:
                        R.op("dve", "sV", lambda E: E.tensor_copy(
                            out=ckvT_v[:, mc, b0:b0 + bn], in_=pmisc[:, 0:bn]))
                    else:
                        R.op("dve", "sV", lambda E: E.tensor_copy(
                            out=kpeT[:, b0:b0 + bn], in_=pmisc[0:RD, 0:bn]))
                    R.wait("te", "sV", R.c["sV"])
                proj_cols(wkva, mc, KD,
                          lambda k, b0, bn: xnT_v[:, k, b0:b0 + bn], keys_n, evac_ckv)

            for mc in range(QLD):
                def evac_ql(blk, b0, bn, mc=mc):
                    R.wait("dve", "sT", R.c["sT"])
                    R.op("dve", "sV", lambda E: E.tensor_copy(
                        out=qlatT_v[:, mc, b0:b0 + bn], in_=pmisc[:, 0:bn]))
                    R.wait("te", "sV", R.c["sV"])
                proj_cols(wqa, mc, KD,
                          lambda k, b0, bn, own0=own0: xnT_v[:, k, own0 + b0:own0 + b0 + bn],
                          own_n, evac_ql)

            swap_mask = []
            for i in range(16):
                swap_mask += [2 * i + 1, 2 * i]
            R.wq("dve", "sQs", 0, c_ropeC)
            R.wq("dve", "sQs", 1, c_ropeS)
            R.op("dve", "sV", lambda E, keys_n=keys_n: E.stream_shuffle(
                out=kpesw[:, 0:keys_n], in_=kpeT[:, 0:keys_n], mask=swap_mask))
            R.wait("dve", "sV", R.c["sV"])
            R.op("dve", "sV", lambda E, keys_n=keys_n: E.tensor_tensor(
                out=kpeT[:, 0:keys_n], in0=kpeT[:, 0:keys_n], in1=ropeC[:, 0:keys_n],
                op=ALU.mult))
            R.op("dve", "sV", lambda E, keys_n=keys_n: E.tensor_tensor(
                out=kpesw[:, 0:keys_n], in0=kpesw[:, 0:keys_n], in1=ropeS[:, 0:keys_n],
                op=ALU.mult))
            R.wait("dve", "sV", R.c["sV"])
            R.op("dve", "sV", lambda E, keys_n=keys_n: E.tensor_tensor(
                out=kpeRT[:, 0:keys_n], in0=kpeT[:, 0:keys_n], in1=kpesw[:, 0:keys_n],
                op=ALU.add))

            for hb in range(H // 4):
                R.wait("dve", "sT", R.c["sT"])
                for mc in range(CKVD):
                    fb = rtile[mc % 2]
                    R.wait("act", "sV", R.c["sV"])
                    R.dma("sQa", mc % 2, fb[:], wkvv[mc, hb])
                    R.wq("dve", "sQa", mc % 2, R.qc("sQa", mc % 2))
                    R.op("dve", "sV", lambda E, fb=fb, mc=mc: E.tensor_copy(
                        out=wring[mc][:, 0:512], in_=fb[:]))
                R.wait("te", "sV", R.c["sV"])
                for kt in range(nt_keys):
                    for mc in range(CKVD):
                        R.op("te", "sT", lambda E, mc=mc, kt=kt: E.matmul(
                            out=pmisc[:], lhsT=ckvT_v[:, mc, 128 * kt:128 * (kt + 1)],
                            rhs=wring[mc][:, 0:512], start=(mc == 0), stop=(mc == CKVD - 1)))
                    R.wait("dve", "sT", R.c["sT"])
                    R.op("dve", "sV", lambda E: E.tensor_copy(
                        out=p1buf[:, 0:512], in_=pmisc[:]))
                    R.wait("pool", "sV", R.c["sV"])
                    for j in range(4):
                        R.dma("sQp", j, v_sc[kt, 4 * hb + j], p1buf[:, 128 * j:128 * (j + 1)],
                              wait_first=False)
                    R.wait("te", "sV", R.c["sV"])
                    for j in range(4):
                        R.wq("dve", "sQp", j, R.qc("sQp", j))

            for hh in range(HCD):
                for kb in range(KD // 4):
                    fb = rtile[(hh * (KD // 4) + kb) % 2]
                    R.wait("act", "sV", R.c["sV"])
                    R.dma("sQa", (hh * (KD // 4) + kb) % 2, fb[:], wo_in[hh, kb])
                    R.wq("dve", "sQa", (hh * (KD // 4) + kb) % 2,
                         R.qc("sQa", (hh * (KD // 4) + kb) % 2))
                    R.op("dve", "sV", lambda E, fb=fb: E.tensor_copy(
                        out=p1buf[:, 0:512], in_=fb[:]))
                    R.wait("pool", "sV", R.c["sV"])
                    R.dma("sQp", 0, wo_bf[hh, kb], p1buf[:, 0:512], wait_first=False)
                    R.wq("dve", "sQp", 0, R.qc("sQp", 0))
            R.barrier()
        A2b.close()

        A3 = ExitStack()
        kfullT = A3.enter_context(nc.sbuf_tensor("kfullT", [128, S], BF16))
        vhb = [A3.enter_context(nc.sbuf_tensor(f"vh{i}", [128, NT * VD], BF16)) for i in range(2)]
        qTh = A3.enter_context(nc.sbuf_tensor("qTh", [128, TSPLIT], BF16))
        qpe = A3.enter_context(nc.sbuf_tensor("qpe", [RD, TSPLIT], F32))
        qpesw = A3.enter_context(nc.sbuf_tensor("qpesw", [RD, TSPLIT], F32))
        probs = A3.enter_context(nc.sbuf_tensor("probs", [128, S], BF16))
        probsT = A3.enter_context(nc.sbuf_tensor("probsT", [128, S], BF16))
        smax = A3.enter_context(nc.sbuf_tensor("smax", [128, 8], F32))
        ssum = A3.enter_context(nc.sbuf_tensor("ssum", [128, 8], F32))
        attb = [A3.enter_context(nc.sbuf_tensor(f"attb{i}", [128, HC], BF16)) for i in range(2)]
        attT = A3.enter_context(nc.sbuf_tensor("attT", [128, HC], BF16))
        astage = A3.enter_context(nc.sbuf_tensor("astage", [128, VD], BF16))

        for role in (0, 1):
            R = recs[role]
            own0, own_n, keys_n = roles[role]
            nt_keys = keys_n // 128
            own_tiles = own_n // 128
            qlatT_v = qlatT.rearrange("p (k s) -> p k s", s=TSPLIT)
            ckvT_v = ckvT.rearrange("p (k s) -> p k s", s=S)
            swap_mask = []
            for i in range(16):
                swap_mask += [2 * i + 1, 2 * i]

            for h in range(H):
                R.wait("dve", "sT", R.c["sT"])
                for mc in range(QLD):
                    fb = rtile[mc % 2]
                    R.wait("act", "sV", R.c["sV"])
                    R.dma("sQa", mc % 2, fb[:, 0:128], wqb[mc, h])
                    R.wq("dve", "sQa", mc % 2, R.qc("sQa", mc % 2))
                    R.op("dve", "sV", lambda E, fb=fb, mc=mc: E.tensor_copy(
                        out=wcol[:, 128 * mc:128 * (mc + 1)] if False else
                        wring[mc % 4][:, 128 * (mc // 4):128 * (mc // 4) + 128],
                        in_=fb[:, 0:128]))
                R.wait("te", "sV", R.c["sV"])
                for blk in range(ceil_div(own_n, 512)):
                    b0 = 512 * blk
                    bn = min(512, own_n - b0)
                    for mc in range(QLD):
                        R.op("te", "sT", lambda E, mc=mc, b0=b0, bn=bn: E.matmul(
                            out=pmisc[:, 0:bn],
                            lhsT=wring[mc % 4][:, 128 * (mc // 4):128 * (mc // 4) + 128],
                            rhs=qlatT_v[:, mc, b0:b0 + bn],
                            start=(mc == 0), stop=(mc == QLD - 1)))
                    R.wait("dve", "sT", R.c["sT"])
                    R.op("dve", "sV", lambda E, b0=b0, bn=bn: E.tensor_copy(
                        out=qTh[0:ND, b0:b0 + bn], in_=pmisc[0:ND, 0:bn]))
                    R.op("dve", "sV", lambda E, b0=b0, bn=bn: E.tensor_copy(
                        out=qpe[:, b0:b0 + bn], in_=pmisc[ND:ND + RD, 0:bn]))
                    R.wait("te", "sV", R.c["sV"])
                R.op("dve", "sV", lambda E, own_n=own_n: E.stream_shuffle(
                    out=qpesw[:, 0:own_n], in_=qpe[:, 0:own_n], mask=swap_mask))
                R.wait("dve", "sV", R.c["sV"])
                R.op("dve", "sV", lambda E, own0=own0, own_n=own_n: E.tensor_tensor(
                    out=qpe[:, 0:own_n], in0=qpe[:, 0:own_n],
                    in1=ropeC[:, own0:own0 + own_n], op=ALU.mult))
                R.op("dve", "sV", lambda E, own0=own0, own_n=own_n: E.tensor_tensor(
                    out=qpesw[:, 0:own_n], in0=qpesw[:, 0:own_n],
                    in1=ropeS[:, own0:own0 + own_n], op=ALU.mult))
                R.wait("dve", "sV", R.c["sV"])
                R.op("dve", "sV", lambda E, own_n=own_n: E.tensor_tensor(
                    out=qTh[ND:ND + RD, 0:own_n], in0=qpe[:, 0:own_n],
                    in1=qpesw[:, 0:own_n], op=ALU.add))

                R.wait("dve", "sT", R.c["sT"])
                for mc in range(CKVD):
                    fb = rtile[mc % 2]
                    R.wait("act", "sV", R.c["sV"])
                    R.dma("sQa", mc % 2, fb[:, 0:ND], wkvk[mc, h])
                    R.wq("dve", "sQa", mc % 2, R.qc("sQa", mc % 2))
                    R.op("dve", "sV", lambda E, fb=fb, mc=mc: E.tensor_copy(
                        out=wring[mc][:, 0:ND], in_=fb[:, 0:ND]))
                R.wait("te", "sV", R.c["sV"])
                for blk in range(ceil_div(keys_n, 512)):
                    b0 = 512 * blk
                    bn = min(512, keys_n - b0)
                    for mc in range(CKVD):
                        R.op("te", "sT", lambda E, mc=mc, b0=b0, bn=bn: E.matmul(
                            out=pmisc[0:ND, 0:bn], lhsT=wring[mc][:, 0:ND],
                            rhs=ckvT_v[:, mc, b0:b0 + bn],
                            start=(mc == 0), stop=(mc == CKVD - 1)))
                    R.wait("dve", "sT", R.c["sT"])
                    R.op("dve", "sV", lambda E, b0=b0, bn=bn: E.tensor_copy(
                        out=kfullT[0:ND, b0:b0 + bn], in_=pmisc[0:ND, 0:bn]))
                    R.op("dve", "sV", lambda E, b0=b0, bn=bn: E.tensor_copy(
                        out=kfullT[ND:128, b0:b0 + bn], in_=kpeRT[:, b0:b0 + bn]))
                    R.wait("te", "sV", R.c["sV"])

                vb = vhb[h % 2]
                R.wait("pool", "sT", R.c["sT"])
                for kt in range(nt_keys):
                    R.dma("sQp", kt % 2, vb[:, 128 * kt:128 * (kt + 1)], v_sc[kt, h],
                          wait_first=False)
                c_v = (R.qc("sQp", 0), R.qc("sQp", 1))

                for qt in range(own_tiles):
                    qtg = own0 // 128 + qt
                    nkeys = (qtg + 1) * 128
                    nkb = ceil_div(nkeys, 512)
                    R.wait("te", "sV", R.c["sV"])
                    for kb in range(nkb):
                        b0 = 512 * kb
                        bn = min(512, nkeys - b0)
                        R.op("te", "sT", lambda E, kb=kb, b0=b0, bn=bn, qt=qt: E.matmul(
                            out=ps[kb][:, 0:bn], lhsT=qTh[:, 128 * qt:128 * (qt + 1)],
                            rhs=kfullT[:, b0:b0 + bn], start=True, stop=True))
                    R.wait("dve", "sT", R.c["sT"])
                    dkb = (nkeys - 128) // 512
                    doff = (nkeys - 128) - 512 * dkb
                    R.op("dve", "sV", lambda E, dkb=dkb, doff=doff: E.tensor_tensor(
                        out=ps[dkb][:, doff:doff + 128], in0=ps[dkb][:, doff:doff + 128],
                        in1=maskS[:], op=ALU.add))
                    R.wait("dve", "sV", R.c["sV"])
                    for kb in range(nkb):
                        b0 = 512 * kb
                        bn = min(512, nkeys - b0)
                        R.op("dve", "sV", lambda E, kb=kb, bn=bn: E.reduce_max(
                            out=smax[:, kb:kb + 1], in_=ps[kb][:, 0:bn], axis=AX.X))
                    R.wait("dve", "sV", R.c["sV"])
                    R.op("dve", "sV", lambda E, nkb=nkb: E.reduce_max(
                        out=smax[:, 6:7], in_=smax[:, 0:nkb], axis=AX.X))
                    R.wait("dve", "sV", R.c["sV"])
                    R.op("dve", "sV", lambda E: E.tensor_scalar(
                        out=smax[:, 7:8], in0=smax[:, 6:7], scalar1=-sm_scale, scalar2=None,
                        op0=ALU.mult))
                    R.wait("act", "sV", R.c["sV"])
                    for kb in range(nkb):
                        b0 = 512 * kb
                        bn = min(512, nkeys - b0)
                        R.op("act", "sA", lambda E, kb=kb, b0=b0, bn=bn: E.activation(
                            out=probs[:, b0:b0 + bn], in_=ps[kb][:, 0:bn], func=AF.Exp,
                            scale=sm_scale, bias=smax[:, 7:8],
                            accum_out=ssum[:, kb:kb + 1]))
                    R.wait("dve", "sA", R.c["sA"])
                    R.op("dve", "sV", lambda E, nkb=nkb: E.reduce_sum(
                        out=ssum[:, 6:7], in_=ssum[:, 0:nkb], axis=AX.X))
                    R.wait("dve", "sV", R.c["sV"])
                    R.op("dve", "sV", lambda E: E.reciprocal(
                        out=ssum[:, 7:8], in_=ssum[:, 6:7]))
                    ntk = qtg + 1
                    R.wait("te", "sA", R.c["sA"])
                    nq = ceil_div(ntk, 4)
                    probsT_v = probsT.rearrange("p (a b) -> p a b", b=128)
                    for q in range(nq):
                        jn = min(4, ntk - 4 * q)
                        for j4 in range(jn):
                            j = 4 * q + j4
                            R.op("te", "sT", lambda E, j=j, j4=j4: E.transpose(
                                out=psbf[:, 128 * j4:128 * (j4 + 1)],
                                in_=probs[:, 128 * j:128 * (j + 1)], identity=ident[:]))
                        eng = "act" if q % 2 == 0 else "dve"
                        R.wait(eng, "sT", R.c["sT"])
                        if eng == "act":
                            R.op("act", "sA", lambda E, q=q, jn=jn: E.activation(
                                out=probsT_v[:, 4 * q:4 * q + jn, :],
                                in_=psbf[:, 0:128 * jn].rearrange("p (a b) -> p a b", b=128),
                                func=AF.Copy))
                        else:
                            R.op("dve", "sV", lambda E, q=q, jn=jn: E.tensor_copy(
                                out=probsT_v[:, 4 * q:4 * q + jn, :],
                                in_=psbf[:, 0:128 * jn].rearrange("p (a b) -> p a b", b=128)))
                        R.wait("te", "sA" if eng == "act" else "sV",
                               R.c["sA"] if eng == "act" else R.c["sV"])
                    R.wq("te", "sQp", 0, c_v[0])
                    R.wq("te", "sQp", 1, c_v[1])
                    for kt in range(ntk):
                        R.op("te", "sT", lambda E, kt=kt, vb=vb, ntk=ntk: E.matmul(
                            out=pav[:, 0:VD], lhsT=probsT[:, 128 * kt:128 * (kt + 1)],
                            rhs=vb[:, 128 * kt:128 * (kt + 1)],
                            start=(kt == 0), stop=(kt == ntk - 1)))
                    R.wait("dve", "sT", R.c["sT"])
                    R.wait("dve", "sV", R.c["sV"])
                    R.op("dve", "sV", lambda E: E.tensor_scalar(
                        out=astage[:], in0=pav[:, 0:VD], scalar1=ssum[:, 7:8], scalar2=None,
                        op0=ALU.mult))
                    R.wait("pool", "sV", R.c["sV"])
                    R.dma("sQp", 2, attn_sc[qtg, h], astage[:], wait_first=False)
                    R.wait("te", "sV", R.c["sV"])
                    R.wq("dve", "sQp", 2, R.qc("sQp", 2))
            R.barrier()

        for role in (0, 1):
            R = recs[role]
            own0, own_n, keys_n = roles[role]
            own_tiles = own_n // 128
            attT_v = attT.rearrange("p (a b) -> p a b", b=128)
            for tt in range(own_tiles):
                gt0 = own0 // 128 + tt
                ab = attb[tt % 2]
                if tt >= 2:
                    R.wait("pool", "sT", R.c["sT"])
                R.dma("sQp", tt % 2, ab[:], attn_sc[gt0].rearrange("h p v -> p h v"))
                R.wq("te", "sQp", tt % 2, R.qc("sQp", tt % 2))
                transpose_quads(
                    R, HCD,
                    lambda j, ab=ab: ab[:, 128 * j:128 * (j + 1)],
                    lambda q: attT_v[:, 4 * q:4 * q + 4, :])
                for ob in range(D // 512):
                    R.wait("act", "sV", R.c["sV"])
                    R.dma("sQa", 0, rtile[0][:],
                          resid_d[128 * gt0:128 * (gt0 + 1), 512 * ob:512 * (ob + 1)])
                    c_r = R.qc("sQa", 0)
                    for hh in range(HCD):
                        wb = wring[hh % 16]
                        R.wait("sync", "sT", R.last_mark.get("sT", 0))
                        R.dma("sQs", hh % 4, wb[:, 0:512], wo_bf[hh, ob])
                        R.wq("te", "sQs", hh % 4, R.qc("sQs", hh % 4))
                        R.op("te", "sT", lambda E, wb=wb, hh=hh: E.matmul(
                            out=pmisc[:], lhsT=attT_v[:, hh, :], rhs=wb[:, 0:512],
                            start=(hh == 0), stop=(hh == HCD - 1)))
                    R.mark("sT")
                    R.wait("dve", "sT", R.c["sT"])
                    R.wq("dve", "sQa", 0, c_r)
                    R.op("dve", "sV", lambda E: E.tensor_tensor(
                        out=rtile[0][:], in0=pmisc[:], in1=rtile[0][:], op=ALU.add))
                    R.wait("act", "sV", R.c["sV"])
                    R.dma("sQa", 0, resid_d[128 * gt0:128 * (gt0 + 1), 512 * ob:512 * (ob + 1)],
                          rtile[0][:], wait_first=False)
                    R.wq("dve", "sQa", 0, R.qc("sQa", 0))
            R.barrier()

        if dbg == "att0":
            for role in (0, 1):
                R = recs[role]
                own0, own_n, keys_n = roles[role]
                for tt in range(own_n // 128):
                    gt0 = own0 // 128 + tt
                    R.dma("sQp", tt % 4, out_d[128 * tt:128 * (tt + 1), 0:H * VD],
                          attn_sc[gt0].rearrange("h p v -> p h v"))
                R.barrier()
            A3.close()
            A2a.close()
            finish(nc, recs)
            return nc

        if dbg == "p4":
            for role in (0, 1):
                R = recs[role]
                own0, own_n, keys_n = roles[role]
                dump_resid(R, own0 if role == 0 else S - TSPLIT)
            A3.close()
            A2a.close()
            finish(nc, recs)
            return nc
        A3.close()
        A2a.close()

        A5 = ExitStack()
        xhatT = A5.enter_context(nc.sbuf_tensor("xhatT5", [128, KD * GN], BF16))
        hbuf = [A5.enter_context(nc.sbuf_tensor(f"h5{i}", [128, DFF], BF16)) for i in range(4)]
        hhatT = A5.enter_context(nc.sbuf_tensor("hhatT5", [128, FD * GN], BF16))
        for role in (0, 1):
            R = recs[role]
            own0, own_n, keys_n = roles[role]
            for gi, (g0, gnn) in enumerate(groups_of(own0, own_n)):
                emit_ffn_group(R, g0, gnn, False, resid_d, ("final", own0))
            R.barrier()
        A5.close()

        finish(nc, recs)

    return nc


def finish(nc, recs):
    with nc.Block() as block:
        def body_for(eng_name):
            def body(E):
                r = E.partition_id()
                p = E.compute_val(r % 2)
                for arm in E.Switch(p, 2):
                    for f in recs[arm].q[eng_name]:
                        f(E)
            return body
        block.tensor(body_for("te"))
        block.vector(body_for("dve"))
        block.scalar(body_for("act"))
        block.gpsimd(body_for("pool"))
        block.sync(body_for("sync"))



def _tile2(a, p=128, f=128):
    K, M = a.shape
    return np.ascontiguousarray(
        a.reshape(K // p, p, M // f, f).transpose(0, 2, 1, 3))


def _slab(a, f=512):
    return _tile2(a, 128, f)


def _stage(inputs, cfg):
    D, DFF, S = cfg["D"], cfg["DFF"], cfg["S"]
    QL, CKV, H = cfg["QL"], cfg["CKV"], cfg["H"]
    TSPLIT = cfg["TSPLIT"]
    KD, CKVD = D // 128, CKV // 128

    Wqa = np.asarray(inputs["L0_attn_Wqa"], np.float32)
    Wqb = np.asarray(inputs["L0_attn_Wqb"], np.float32)
    Wkva = np.asarray(inputs["L0_attn_Wkva"], np.float32)
    Wkvb = np.asarray(inputs["L0_attn_Wkvb"], np.float32)
    Wo = np.asarray(inputs["L0_attn_Wo"], np.float32)
    w_fc = np.asarray(inputs["L0_ffn_w_fc"], np.float32)
    w_proj = np.asarray(inputs["L0_ffn_w_proj"], np.float32)
    x = np.asarray(inputs["L0_attn_hidden_states"], np.float32)

    wfcT = _slab(np.ascontiguousarray(w_fc.T))
    wprT = _slab(np.ascontiguousarray(w_proj.T))
    wqa_t = _tile2(Wqa)
    wqb_t = _tile2(Wqb)
    wkva_p = np.zeros((D, (CKVD + 1) * 128), np.float32)
    wkva_p[:, :CKV + RD] = Wkva
    wkva_t = _tile2(wkva_p)
    wkvb3 = Wkvb.reshape(CKV, H, ND + VD)
    wkvk_t = np.ascontiguousarray(
        wkvb3[:, :, :ND].reshape(CKVD, 128, H, ND).transpose(0, 2, 1, 3))
    wkvv_t = np.ascontiguousarray(
        wkvb3[:, :, ND:].reshape(CKVD, 128, H // 4, 4, VD)
        .transpose(0, 2, 1, 3, 4).reshape(CKVD, H // 4, 128, 4 * VD))
    wo_t = _slab(Wo)

    ident = np.eye(128, dtype=ml_dtypes.bfloat16)
    mask = np.triu(np.full((128, 128), -1.0e30, np.float32), 1)
    half = RD // 2
    inv = 1.0 / (10000.0 ** (np.arange(half, dtype=np.float64) * 2.0 / RD))
    tpos = np.arange(S, dtype=np.float64)
    fr = np.outer(inv, tpos)
    cosr = np.repeat(np.cos(fr), 2, axis=0).astype(np.float32)
    sr = np.empty((RD, S), np.float64)
    sr[0::2] = -np.sin(fr)
    sr[1::2] = np.sin(fr)
    ropeC = cosr.astype(ml_dtypes.bfloat16)
    ropeS = sr.astype(np.float32).astype(ml_dtypes.bfloat16)

    shared = dict(wfcT=wfcT, wprT=wprT, wqa=wqa_t, wqb=wqb_t, wkva=wkva_t,
                  wkvk=wkvk_t, wkvv=wkvv_t, wo_in=wo_t, ident_c=ident,
                  mask_c=mask, ropeC_c=ropeC, ropeS_c=ropeS)
    in_maps = []
    for core in range(8):
        b = core // 2
        m = dict(shared)
        m["x_in"] = np.ascontiguousarray(x[b])
        in_maps.append(m)
    return in_maps


def kernel(**inputs):
    cfg = CFG
    x = np.asarray(inputs["L0_attn_hidden_states"], np.float32)
    B, S, D = x.shape
    in_maps = _stage(inputs, cfg)
    nc = build_nc(cfg)
    res = run_bass_kernel_spmd(nc, in_maps, core_ids=list(range(8)))
    out = np.empty((B, S, D), np.float32)
    TS = cfg["TSPLIT"]
    for b in range(B):
        out[b, :TS] = res.results[2 * b]["out"][:TS]
        out[b, TS:] = res.results[2 * b + 1]["out"][:S - TS]
    return out
